# version 02 (final)
# baseline (speedup 1.0000x reference)
import sys
import contextlib
import numpy as np

sys.path.insert(0, "/opt/trn_rl_repo")
import concourse.bass as bass
import concourse.bacc as bacc_mod
import concourse.mybir as mybir
from concourse.bass_utils import run_bass_kernel_spmd

F32 = mybir.dt.float32
AF = mybir.ActivationFunctionType
ALU = mybir.AluOpType
AX = mybir.AxisListType

NCORES = 8
B, T, C = 2, 16384, 1024
NT = B * T
TPC = NT // NCORES
TILES = TPC // 128


class LV:
    def __init__(self, depth):
        self.depth = depth


class P:

    def __init__(self, nc, sem):
        self.nc = nc
        self.sem = sem
        self.delta = 0
        self.started = False
        self.dry = False
        self.cb = self._regmap(nc.alloc_registers("cbreg"))
        self.wv = self._regmap(nc.alloc_registers("wvreg"))
        self.t1 = self._regmap(nc.alloc_registers("t1reg"))
        self.o1 = self._regmap(nc.alloc_registers("o1reg"))
        self.o2 = self._regmap(nc.alloc_registers("o2reg"))
        nc.regs_mov(self._raw_cb, 0)
        self.lv = {}
        self.depth = 0
        self._ocnt = 0

    def _regmap(self, handles):
        m = {}
        for r in handles:
            m[r.engine] = r
        m["_raw"] = handles
        return m

    @property
    def _raw_cb(self):
        return self.cb["_raw"]

    def S(self, engname, emit, inc=1):
        if self.dry:
            self.delta += inc
            return
        eng = getattr(self.nc, engname)
        et = eng.engine
        if self.started:
            if self.delta == 0:
                eng.wait_ge(self.sem, self.cb[et])
            else:
                eng.reg_add(self.wv[et], self.cb[et], self.delta)
                eng.wait_ge(self.sem, self.wv[et])
        self._ocnt = 0

        def off(terms, const=0):
            const = const + sum(v * c for v, c in terms if isinstance(v, int))
            terms = [(v, c) for v, c in terms
                     if c != 0 and not isinstance(v, int)]
            if not terms:
                return const
            oreg = (self.o1 if self._ocnt == 0 else self.o2)[et]
            self._ocnt += 1
            first = True
            for v, cc in terms:
                lvreg = self.lv[v.depth][1][et]
                if cc == 1 and first:
                    eng.reg_add(oreg, lvreg, const)
                    first = False
                    continue
                eng.reg_mul(self.t1[et], lvreg, cc)
                if first:
                    eng.reg_add(oreg, self.t1[et], const)
                    first = False
                else:
                    eng.reg_add(oreg, oreg, self.t1[et])
            return oreg

        ins = emit(eng, off)
        if not isinstance(ins, (list, tuple)):
            ins = [ins]
        ins[-1].then_inc(self.sem, inc)
        self.delta += inc
        self.started = True

    def dma(self, engname, out_fn, in_fn):
        self.S(engname, lambda e, o: e.dma_start(out_fn(o), in_fn(o)), inc=16)

    def loop(self, n, body):
        nc = self.nc
        d = self.depth
        if d not in self.lv:
            h = nc.alloc_registers(f"lvreg{d}")
            self.lv[d] = (h, self._regmap(h))
        raw = self.lv[d][0]
        lid = nc.next_id()
        lstart, lend = f"lc{lid}_loop", f"lc{lid}_end"
        nc.regs_mov(raw, 0)
        nc.br(lstart)
        d0 = self.delta
        with nc.body(lstart):
            self.depth = d + 1
            body(LV(d))
            self.depth = d
            citer = self.delta - d0
            if citer:
                nc.regs_alu(self._raw_cb, self._raw_cb, citer, op=ALU.add)
            nc.regs_alu(raw, raw, 1, op=ALU.add)
            nc.br_lt(raw, n, on_true=lstart, on_false=lend)
        nc.switch_bb(lend)
        self.delta = d0

    def final_wait(self, engname):
        eng = getattr(self.nc, engname)
        et = eng.engine
        if self.delta == 0:
            eng.wait_ge(self.sem, self.cb[et])
        else:
            eng.reg_add(self.wv[et], self.cb[et], self.delta)
            eng.wait_ge(self.sem, self.wv[et])



def mk_ident(p, ident_f, ones_row, ones_col, ut=None):
    p.S("gpsimd", lambda g, o: g.memset(ident_f[:, :], 1.0))
    p.S("gpsimd", lambda g, o: g.affine_select(
        ident_f[:, :], ident_f[:, :], [[1, 128]], ALU.is_equal, 0.0,
        base=0, channel_multiplier=-1))
    p.S("gpsimd", lambda g, o: g.memset(ones_row[:, :], 1.0))
    p.S("gpsimd", lambda g, o: g.memset(ones_col[:, :], 1.0))
    if ut is not None:
        p.S("gpsimd", lambda g, o: g.memset(ut[:, :], 1.0))
        p.S("gpsimd", lambda g, o: g.affine_select(
            ut[:, :], ut[:, :], [[1, 128]], ALU.is_gt, 0.0,
            base=0, channel_multiplier=-1))


def bcast_row(p, row_ap_fn, out_sb_fn, n, ones_row, pbc):
    o = 0
    while o < n:
        w = min(512, n - o)
        p.S("tensor", lambda t, _o, o=o, w=w: t.matmul(
            pbc[:, 0:w], ones_row[0:1, :], row_ap_fn(o, w),
            start=True, stop=True))
        p.S("vector", lambda v, _o, o=o, w=w: v.tensor_copy(
            out_sb_fn(o, w), pbc[:, 0:w]))
        o += 512


def layernorm(p, x_sb, out_sb, st, D, eps=1e-5, sq_sb=None,
              gamma_bc=None, beta_bc=None):
    if sq_sb is None:
        sq_sb = out_sb
    p.S("vector", lambda v, o: v.reduce_sum(st[:, 0:1], x_sb[:, 0:D], axis=AX.X))
    p.S("vector", lambda v, o: v.tensor_scalar_mul(st[:, 1:2], st[:, 0:1], 1.0 / D))
    p.S("vector", lambda v, o: v.tensor_scalar(out_sb[:, 0:D], x_sb[:, 0:D],
                                               st[:, 1:2], None, ALU.subtract))
    p.S("scalar", lambda s, o: s.activation(sq_sb[:, 0:D], out_sb[:, 0:D], AF.Square))
    p.S("vector", lambda v, o: v.reduce_sum(st[:, 2:3], sq_sb[:, 0:D], axis=AX.X))
    p.S("vector", lambda v, o: v.tensor_scalar(st[:, 3:4], st[:, 2:3], 1.0 / D, eps,
                                               ALU.mult, ALU.add))
    p.S("scalar", lambda s, o: s.activation(st[:, 4:5], st[:, 3:4], AF.Ln))
    p.S("scalar", lambda s, o: s.activation(st[:, 5:6], st[:, 4:5], AF.Exp, scale=-0.5))
    p.S("vector", lambda v, o: v.tensor_scalar(out_sb[:, 0:D], out_sb[:, 0:D],
                                               st[:, 5:6], None, ALU.mult))
    if gamma_bc is not None:
        p.S("vector", lambda v, o: v.tensor_mul(out_sb[:, 0:D], out_sb[:, 0:D],
                                                gamma_bc))
    if beta_bc is not None:
        p.S("vector", lambda v, o: v.tensor_add(out_sb[:, 0:D], out_sb[:, 0:D],
                                                beta_bc))


def transpose8(p, src, dst, ident_f, ptf, nchunks=8):
    for j in range(nchunks):
        p.S("tensor", lambda t, o, j=j: t.transpose(
            ptf[:, :], src[:, j * 128:(j + 1) * 128], ident_f[:, :]))
        p.S("vector", lambda v, o, j=j: v.tensor_copy(
            dst[:, j * 128:(j + 1) * 128], ptf[:, :]))


def mm_acc(p, psum_ap, lhsT_fn, rhs_fn, K):
    def emit(t, o):
        out = []
        for j in range(K):
            out.append(t.matmul(psum_ap, lhsT_fn(j), rhs_fn(j),
                                start=(j == 0), stop=(j == K - 1)))
        return out
    p.S("tensor", emit)



def build_l0front():
    nc = bacc_mod.Bacc("TRN2", target_bir_lowering=False)
    x_ext = nc.dram_tensor("x_ext", [TPC + 1, 1024], F32, kind="ExternalInput")
    wT = {nm: nc.dram_tensor(nm, [1024, 1024], F32, kind="ExternalInput")
          for nm in ("WkT", "WvT", "WrT", "WgT")}
    alor_d = nc.dram_tensor("alor", [1024, 160], F32, kind="ExternalInput")
    blor_d = nc.dram_tensor("blor", [160, 1024], F32, kind="ExternalInput")
    tda_d = nc.dram_tensor("tda", [1024, 64], F32, kind="ExternalInput")
    tdb_d = nc.dram_tensor("tdb", [64, 1024], F32, kind="ExternalInput")
    rows_d = nc.dram_tensor("rows", [7, 1024], F32, kind="ExternalInput")
    outs = {nm: nc.dram_tensor(nm, [TPC, 1024], F32, kind="ExternalOutput")
            for nm in ("w_o", "k_o", "v_o", "r_o", "g_o", "xn_o")}
    proj_out = {1: "k_o", 2: "v_o", 3: "r_o", 4: "g_o"}

    with contextlib.ExitStack() as ctx:
        sb = lambda n, s, d=F32: ctx.enter_context(nc.sbuf_tensor(n, s, d))
        ps = lambda n, s, d=F32: ctx.enter_context(nc.psum_tensor(n, s, d))
        wp = sb("wp", [128, 2 * 8192])
        wstream = sb("wstream", [128, 8192])
        alor = sb("alor_s", [128, 8 * 160])
        blor = sb("blor_s", [32, 5 * 1024])
        tda = sb("tda_s", [128, 8 * 64])
        tdb = sb("tdb_s", [64, 1024])
        rows_sb = sb("rows_sb", [1, 1024])
        bc = sb("bc", [128, 7 * 1024])
        xt = sb("xt", [128, 1024])
        xq = sb("xq", [128, 1024])
        xn = sb("xn", [128, 1024])
        xnp = sb("xnp", [128, 1024])
        dxa = sb("dxa", [128, 1024])
        tmp = sb("tmp", [128, 1024])
        xxT = sb("xxT", [128, 1024])
        yout = sb("yout", [128, 1024])
        xxf = xt
        lh = sb("lh", [128, 160])
        lhT = sb("lhT", [32, 128])
        tw = sb("tw", [128, 64])
        twT = sb("twT", [64, 128])
        st = sb("st", [128, 8])
        id_f = sb("id_f", [128, 128])
        o_r = sb("o_r", [1, 128])
        o_c = sb("o_c", [128, 1])
        pa = ps("pa", [128, 512])
        pb = ps("pb", [128, 512])
        pbc = ps("pbc", [128, 512])
        pmm = ps("pmm", [128, 160])
        pw64 = ps("pw64", [128, 64])
        ptf = ps("ptf", [128, 128])
        sem = ctx.enter_context(nc.semaphore("chain"))
        p = P(nc, sem)
        mk_ident(p, id_f, o_r, o_c)
        for wi, nm in enumerate(("WkT", "WvT")):
            def bw(i, wi=wi, nm=nm):
                p.dma("sync",
                      lambda o, wi=wi: bass.AP(wp, o([(i, 1024)], wi * 8192),
                                               [[2 * 8192, 128], [1, 1024]]),
                      lambda o, nm=nm: bass.AP(wT[nm], o([(i, 128 * 1024)]),
                                               [[1024, 128], [1, 1024]]))
            p.loop(8, bw)
        p.loop(8, lambda i: p.dma("sync",
               lambda o: bass.AP(alor, o([(i, 160)]), [[8 * 160, 128], [1, 160]]),
               lambda o: bass.AP(alor_d, o([(i, 128 * 160)]), [[160, 128], [1, 160]])))
        p.loop(5, lambda i: p.dma("sync",
               lambda o: bass.AP(blor, o([(i, 1024)]), [[5 * 1024, 32], [1, 1024]]),
               lambda o: bass.AP(blor_d, o([(i, 32 * 1024)]), [[1024, 32], [1, 1024]])))
        p.loop(8, lambda i: p.dma("sync",
               lambda o: bass.AP(tda, o([(i, 64)]), [[8 * 64, 128], [1, 64]]),
               lambda o: bass.AP(tda_d, o([(i, 128 * 64)]), [[64, 128], [1, 64]])))
        p.dma("sync", lambda o: tdb[:, :], lambda o: tdb_d.ap()[:, :])
        for r in range(7):
            p.dma("sync", lambda o: bass.AP(rows_sb, 0, [[1024, 1], [1, 1024]]),
                  lambda o, r=r: bass.AP(rows_d, r * 1024, [[1024, 1], [1, 1024]]))
            bcast_row(p, lambda off, w: bass.AP(rows_sb, off, [[1024, 1], [1, w]]),
                      lambda off, w, r=r: bass.AP(bc, r * 1024 + off,
                                                  [[7168, 128], [1, w]]),
                      1024, o_r, pbc)
        miu_bc = bass.AP(bc, 0, [[7168, 128], [1, 1024]])

        def tdm_bc(h):
            return bass.AP(bc, 1024 + h * 512, [[7168, 128], [1, 512]])

        def lam_bc(f, cs, cw):
            return bass.AP(bc, 2048 + f * 1024 + cs, [[7168, 128], [1, cw]])

        def tile(i):
            p.dma("sync", lambda o: xt[:, :],
                  lambda o: bass.AP(x_ext, o([(i, 128 * 1024)], 1024),
                                    [[1024, 128], [1, 1024]]))
            p.dma("sync", lambda o: xq[:, :],
                  lambda o: bass.AP(x_ext, o([(i, 128 * 1024)]),
                                    [[1024, 128], [1, 1024]]))
            layernorm(p, xt, xn, st, 1024, sq_sb=tmp)
            layernorm(p, xq, xnp, st, 1024, sq_sb=tmp)
            p.S("vector", lambda v, o: v.tensor_sub(dxa[:, :], xnp[:, :], xn[:, :]))
            p.S("vector", lambda v, o: v.tensor_mul(tmp[:, :], dxa[:, :], miu_bc))
            p.S("vector", lambda v, o: v.tensor_add(tmp[:, :], xn[:, :], tmp[:, :]))
            transpose8(p, tmp, xxT, id_f, ptf)
            mm_acc(p, pmm[:, :],
                   lambda j: xxT[:, j * 128:(j + 1) * 128],
                   lambda j: bass.AP(alor, j * 160, [[1280, 128], [1, 160]]), 8)
            p.S("scalar", lambda s, o: s.activation(lh[:, :], pmm[:, :], AF.Tanh))
            for f in range(5):
                p.S("tensor", lambda t, o, f=f: t.transpose(
                    bass.AP(ptf, 0, [[128, 32], [1, 128]]),
                    lh[:, f * 32:(f + 1) * 32], id_f[:, :]))
                p.S("vector", lambda v, o: v.tensor_copy(
                    lhT[:, :], bass.AP(ptf, 0, [[128, 32], [1, 128]])))
                for half, pp in ((0, pa), (1, pb)):
                    p.S("tensor", lambda t, o, f=f, half=half, pp=pp: t.matmul(
                        pp[:, :], lhT[:, :],
                        bass.AP(blor, f * 1024 + half * 512, [[5120, 32], [1, 512]]),
                        start=True, stop=True))
                for half, pp in ((0, pa), (1, pb)):
                    p.S("vector", lambda v, o, f=f, half=half, pp=pp: v.tensor_add(
                        xxf[:, half * 512:(half + 1) * 512], pp[:, :],
                        lam_bc(f, half * 512, 512)))
                p.S("vector", lambda v, o: v.tensor_mul(xxf[:, :], xxf[:, :], dxa[:, :]))
                p.S("vector", lambda v, o: v.tensor_add(xxf[:, :], xxf[:, :], xn[:, :]))
                transpose8(p, xxf, xxT, id_f, ptf)
                if f == 0:
                    mm_acc(p, pw64[:, :],
                           lambda j: xxT[:, j * 128:(j + 1) * 128],
                           lambda j: bass.AP(tda, j * 64, [[512, 128], [1, 64]]), 8)
                    p.S("scalar", lambda s, o: s.activation(tw[:, :], pw64[:, :], AF.Tanh))
                    p.S("tensor", lambda t, o: t.transpose(
                        bass.AP(ptf, 0, [[128, 64], [1, 128]]), tw[:, :], id_f[:, :]))
                    p.S("vector", lambda v, o: v.tensor_copy(
                        twT[:, :], bass.AP(ptf, 0, [[128, 64], [1, 128]])))
                    for half, pp in ((0, pa), (1, pb)):
                        p.S("tensor", lambda t, o, half=half, pp=pp: t.matmul(
                            pp[:, :], twT[:, :], tdb[:, half * 512:(half + 1) * 512],
                            start=True, stop=True))
                    for half, pp in ((0, pa), (1, pb)):
                        p.S("vector", lambda v, o, half=half, pp=pp: v.tensor_add(
                            yout[:, half * 512:(half + 1) * 512], pp[:, :],
                            tdm_bc(half)))
                    p.dma("gpsimd",
                          lambda o: bass.AP(outs["w_o"], o([(i, 128 * 1024)]),
                                            [[1024, 128], [1, 1024]]),
                          lambda o: yout[:, :])
                else:
                    wi, oname = f - 1, proj_out[f]
                    if f >= 3:
                        nm = "WrT" if f == 3 else "WgT"
                        for j in range(8):
                            p.dma("sync",
                                  lambda o, j=j: bass.AP(wstream, j * 1024,
                                                         [[8192, 128], [1, 1024]]),
                                  lambda o, j=j, nm=nm: bass.AP(
                                      wT[nm], j * (128 * 1024),
                                      [[1024, 128], [1, 1024]]))
                    for half, pp in ((0, pa), (1, pb)):
                        if f >= 3:
                            rhs = lambda j, half=half: bass.AP(
                                wstream, j * 1024 + half * 512,
                                [[8192, 128], [1, 512]])
                        else:
                            rhs = lambda j, wi=wi, half=half: bass.AP(
                                wp, wi * 8192 + j * 1024 + half * 512,
                                [[2 * 8192, 128], [1, 512]])
                        mm_acc(p, pp[:, :],
                               lambda j: xxT[:, j * 128:(j + 1) * 128], rhs, 8)
                    if f == 4:
                        for half, pp in ((0, pa), (1, pb)):
                            p.S("scalar", lambda s, o, half=half, pp=pp: s.activation(
                                yout[:, half * 512:(half + 1) * 512], pp[:, :], AF.Silu))
                    else:
                        for half, pp in ((0, pa), (1, pb)):
                            p.S("vector", lambda v, o, half=half, pp=pp: v.tensor_copy(
                                yout[:, half * 512:(half + 1) * 512], pp[:, :]))
                    p.dma("gpsimd",
                          lambda o, oname=oname: bass.AP(outs[oname], o([(i, 128 * 1024)]),
                                                         [[1024, 128], [1, 1024]]),
                          lambda o: yout[:, :])
            p.dma("gpsimd",
                  lambda o: bass.AP(outs["xn_o"], o([(i, 128 * 1024)]),
                                    [[1024, 128], [1, 1024]]),
                  lambda o: xn[:, :])

        p.loop(TILES, tile)
        p.final_wait("gpsimd")
    nc.finalize()
    return nc


def run_l0front(X):
    nc = build_l0front()
    x = X["L0_attn_x"]
    rows = np.concatenate([
        X["L0_attn_miu_x"].reshape(1, 1024),
        X["L0_attn_td_miu"].reshape(1, 1024),
        X["L0_attn_lambda_"].reshape(5, 1024)], axis=0)
    common = {
        "WkT": np.ascontiguousarray(X["L0_attn_Wk"].T),
        "WvT": np.ascontiguousarray(X["L0_attn_Wv"].T),
        "WrT": np.ascontiguousarray(X["L0_attn_Wr"].T),
        "WgT": np.ascontiguousarray(X["L0_attn_Wg"].T),
        "alor": X["L0_attn_A_lora"],
        "blor": np.ascontiguousarray(X["L0_attn_B_lora"].reshape(160, 1024)),
        "tda": X["L0_attn_td_A"], "tdb": X["L0_attn_td_B"],
        "rows": np.ascontiguousarray(rows),
    }
    in_maps = []
    for c in range(NCORES):
        b, q = c // 4, c % 4
        t0 = q * TPC
        prev = (np.zeros((1, 1024), np.float32) if t0 == 0
                else x[b, t0 - 1:t0])
        xe = np.concatenate([prev, x[b, t0:t0 + TPC]], axis=0)
        m = dict(common)
        m["x_ext"] = np.ascontiguousarray(xe)
        in_maps.append(m)
    res = run_bass_kernel_spmd(nc, in_maps, core_ids=list(range(NCORES)))
    R = res.results
    out = {}
    for key, nm in (("w", "w_o"), ("k", "k_o"), ("v", "v_o"), ("r", "r_o"),
                    ("g", "g_o"), ("xn", "xn_o")):
        out[key] = np.stack([np.stack([R[b * 4 + q][nm] for q in range(4)])
                             for b in range(2)]).reshape(2, 16384, 1024)
    return out



def build_wkv():
    nc = bacc_mod.Bacc("TRN2", target_bir_lowering=False)
    ins = {nm: nc.dram_tensor(nm, [T, 256], F32, kind="ExternalInput")
           for nm in ("r_i", "k_i", "v_i", "w_i")}
    u_d = nc.dram_tensor("u_i", [1, 256], F32, kind="ExternalInput")
    y_d = nc.dram_tensor("y_o", [T, 256], F32, kind="ExternalOutput")
    NCH = T // 128

    with contextlib.ExitStack() as ctx:
        sb = lambda n, s, d=F32: ctx.enter_context(nc.sbuf_tensor(n, s, d))
        ps = lambda n, s, d=F32: ctx.enter_context(nc.psum_tensor(n, s, d))
        rt = sb("rt", [128, 256]); kt = sb("kt", [128, 256])
        vt = sb("vt", [128, 256]); wt = sb("wt", [128, 256])
        ewt = sb("ewt", [128, 256]); S_sb = sb("S_sb", [128, 256])
        ti = sb("ti", [128, 256]); t1sb = sb("t1sb", [128, 256])
        e1 = sb("e1", [128, 256])
        QS = sb("QS", [128, 256]); KS = sb("KS", [128, 256])
        Qi = sb("Qi", [128, 256]); Kst = sb("Kst", [128, 256])
        Hbc = sb("Hbc", [128, 256]); Totbc = sb("Totbc", [128, 256])
        QS_T = sb("QS_T", [128, 256]); KS_T = sb("KS_T", [128, 256])
        Qi_T = sb("Qi_T", [128, 256]); AT_sb = sb("AT_sb", [128, 512])
        ru = sb("ru", [128, 256]); ruk = sb("ruk", [128, 256])
        dc = sb("dc", [128, 4])
        Ysb = sb("Ysb", [128, 256])
        S_state = sb("S_state", [128, 256]); S_tmp = sb("S_tmp", [128, 256])
        SU_sb = sb("SU_sb", [128, 256])
        cLcol = sb("cLcol", [128, 4])
        trow = sb("trow", [1, 256]); hrow = sb("hrow", [1, 256])
        crow = sb("crow", [1, 256]); urow = sb("urow", [1, 256])
        u_bc = sb("u_bc", [128, 256])
        id_f = sb("id_f", [128, 128]); UT = sb("UT", [128, 128])
        o_r = sb("o_r", [1, 128]); o_c = sb("o_c", [128, 1])
        pS = ps("pS", [128, 256]); pO = ps("pO", [1, 256])
        pbc = ps("pbc", [128, 256]); pAT = ps("pAT", [128, 128])
        pY = ps("pY", [128, 256]); pSU = ps("pSU", [64, 256])
        ptf = ps("ptf", [128, 128])
        sem = ctx.enter_context(nc.semaphore("chain"))
        p = P(nc, sem)
        mk_ident(p, id_f, o_r, o_c, ut=UT)
        p.S("gpsimd", lambda g, o: g.memset(S_state[:, :], 0.0))
        p.dma("sync", lambda o: urow[:, :], lambda o: u_d.ap()[:, :])
        p.S("tensor", lambda t, o: t.matmul(pbc[:, :], o_r[0:1, :], urow[0:1, :],
                                            start=True, stop=True))
        p.S("vector", lambda v, o: v.tensor_copy(u_bc[:, :], pbc[:, :]))

        def chunk(i):
            for nm, dst in (("r_i", rt), ("k_i", kt), ("v_i", vt), ("w_i", wt)):
                p.dma("sync", lambda o, dst=dst: dst[:, :],
                      lambda o, nm=nm: bass.AP(ins[nm], o([(i, 128 * 256)]),
                                               [[256, 128], [1, 256]]))
            p.S("scalar", lambda s, o: s.activation(ewt[:, :], wt[:, :], AF.Exp))
            p.S("tensor", lambda t, o: t.matmul(pS[:, :], UT[:, :], ewt[:, :],
                                                start=True, stop=True))
            p.S("vector", lambda v, o: v.tensor_copy(S_sb[:, :], pS[:, :]))
            p.S("vector", lambda v, o: v.tensor_add(ti[:, :], S_sb[:, :], ewt[:, :]))
            p.S("tensor", lambda t, o: t.matmul(pO[0:1, :], o_c[:, 0:1], ewt[:, :],
                                                start=True, stop=True))
            p.S("vector", lambda v, o: v.tensor_copy(trow[:, :], pO[0:1, :]))
            p.S("vector", lambda v, o: v.tensor_scalar_mul(hrow[:, :], trow[:, :], 0.5))
            p.S("tensor", lambda t, o: t.matmul(pbc[:, :], o_r[0:1, :], hrow[0:1, :],
                                                start=True, stop=True))
            p.S("vector", lambda v, o: v.tensor_copy(Hbc[:, :], pbc[:, :]))
            p.S("tensor", lambda t, o: t.matmul(pbc[:, :], o_r[0:1, :], trow[0:1, :],
                                                start=True, stop=True))
            p.S("vector", lambda v, o: v.tensor_copy(Totbc[:, :], pbc[:, :]))
            p.S("vector", lambda v, o: v.tensor_sub(t1sb[:, :], Hbc[:, :], S_sb[:, :]))
            p.S("scalar", lambda s, o: s.activation(e1[:, :], t1sb[:, :], AF.Exp))
            p.S("vector", lambda v, o: v.tensor_mul(QS[:, :], rt[:, :], e1[:, :]))
            p.S("scalar", lambda s, o: s.activation(e1[:, :], S_sb[:, :], AF.Exp,
                                                    scale=-1.0))
            p.S("vector", lambda v, o: v.tensor_mul(Qi[:, :], rt[:, :], e1[:, :]))
            p.S("vector", lambda v, o: v.tensor_sub(t1sb[:, :], ti[:, :], Hbc[:, :]))
            p.S("scalar", lambda s, o: s.activation(e1[:, :], t1sb[:, :], AF.Exp))
            p.S("vector", lambda v, o: v.tensor_mul(KS[:, :], kt[:, :], e1[:, :]))
            p.S("vector", lambda v, o: v.tensor_sub(t1sb[:, :], ti[:, :], Totbc[:, :]))
            p.S("scalar", lambda s, o: s.activation(e1[:, :], t1sb[:, :], AF.Exp))
            p.S("vector", lambda v, o: v.tensor_mul(Kst[:, :], kt[:, :], e1[:, :]))
            p.S("vector", lambda v, o: v.tensor_mul(ru[:, :], rt[:, :], u_bc[:, :]))
            p.S("vector", lambda v, o: v.tensor_mul(ruk[:, :], ru[:, :], kt[:, :]))
            p.S("vector", lambda v, o: v.reduce_sum(
                bass.AP(dc, 0, [[4, 128], [1, 4], [0, 1]]),
                bass.AP(ruk, 0, [[256, 128], [64, 4], [1, 64]]), axis=AX.X))
            for src, dst in ((QS, QS_T), (KS, KS_T), (Qi, Qi_T)):
                for m in range(2):
                    p.S("tensor", lambda t, o, src=src, m=m: t.transpose(
                        ptf[:, :], src[:, m * 128:(m + 1) * 128], id_f[:, :]))
                    p.S("vector", lambda v, o, dst=dst, m=m: v.tensor_copy(
                        dst[:, m * 128:(m + 1) * 128], ptf[:, :]))
            for pp in range(4):
                base, mm = (pp % 2) * 64, pp // 2
                p.S("tensor", lambda t, o, base=base, mm=mm: t.matmul(
                    pAT[:, :], KS_T[base:base + 64, mm * 128:(mm + 1) * 128],
                    QS_T[base:base + 64, mm * 128:(mm + 1) * 128],
                    start=True, stop=True))
                p.S("vector", lambda v, o, pp=pp: v.tensor_mul(
                    AT_sb[:, pp * 128:(pp + 1) * 128], pAT[:, :], UT[:, :]))

            def ymm(t, o):
                out = []
                for pp in range(4):
                    base, mm = (pp % 2) * 64, pp // 2
                    out.append(t.matmul(pY[:, pp * 64:(pp + 1) * 64],
                                        AT_sb[:, pp * 128:(pp + 1) * 128],
                                        vt[:, pp * 64:(pp + 1) * 64],
                                        start=True, stop=False))
                    out.append(t.matmul(pY[:, pp * 64:(pp + 1) * 64],
                                        Qi_T[base:base + 64, mm * 128:(mm + 1) * 128],
                                        S_state[base:base + 64, pp * 64:(pp + 1) * 64],
                                        start=False, stop=True))
                return out
            p.S("tensor", ymm)

            def smm(t, o):
                out = []
                for pp in range(4):
                    out.append(t.matmul(pSU[0:64, pp * 64:(pp + 1) * 64],
                                        Kst[:, pp * 64:(pp + 1) * 64],
                                        vt[:, pp * 64:(pp + 1) * 64],
                                        start=True, stop=True))
                return out
            p.S("tensor", smm)
            p.S("scalar", lambda s, o: s.activation(crow[:, :], trow[:, :], AF.Exp,
                                                    scale=-1.0))
            for pp in range(4):
                p.S("tensor", lambda t, o, pp=pp: t.transpose(
                    bass.AP(ptf, 0, [[128, 64], [1, 1]]),
                    crow[0:1, pp * 64:(pp + 1) * 64], id_f[0:1, 0:1]))
                p.S("vector", lambda v, o, pp=pp: v.tensor_copy(
                    cLcol[(pp % 2) * 64:(pp % 2) * 64 + 64, pp:pp + 1],
                    bass.AP(ptf, 0, [[128, 64], [1, 1]])))
            for pp in range(4):
                p.S("vector", lambda v, o, pp=pp: v.tensor_copy(
                    SU_sb[(pp % 2) * 64:(pp % 2) * 64 + 64, pp * 64:(pp + 1) * 64],
                    pSU[0:64, pp * 64:(pp + 1) * 64]))
            for pp in range(4):
                b0 = (pp % 2) * 64
                p.S("vector", lambda v, o, pp=pp, b0=b0: v.scalar_tensor_tensor(
                    S_state[b0:b0 + 64, pp * 64:(pp + 1) * 64],
                    S_state[b0:b0 + 64, pp * 64:(pp + 1) * 64],
                    cLcol[b0:b0 + 64, pp:pp + 1],
                    SU_sb[b0:b0 + 64, pp * 64:(pp + 1) * 64],
                    ALU.mult, ALU.add))
            p.S("vector", lambda v, o: v.tensor_copy(Ysb[:, :], pY[:, :]))
            for pp in range(4):
                p.S("vector", lambda v, o, pp=pp: v.scalar_tensor_tensor(
                    Ysb[:, pp * 64:(pp + 1) * 64], vt[:, pp * 64:(pp + 1) * 64],
                    dc[:, pp:pp + 1], Ysb[:, pp * 64:(pp + 1) * 64],
                    ALU.mult, ALU.add))
            p.dma("gpsimd", lambda o: bass.AP(y_d, o([(i, 128 * 256)]),
                                              [[256, 128], [1, 256]]),
                  lambda o: Ysb[:, :])

        p.loop(NCH, chunk)
        p.final_wait("gpsimd")
    nc.finalize()
    return nc


def run_wkv(lb, X):
    nc = build_wkv()
    u = X["L0_attn_u"].reshape(16, 64)
    in_maps = []
    for c in range(NCORES):
        b, hq = c // 4, c % 4
        cs = hq * 256
        in_maps.append({
            "r_i": np.ascontiguousarray(lb["r"][b, :, cs:cs + 256]),
            "k_i": np.ascontiguousarray(lb["k"][b, :, cs:cs + 256]),
            "v_i": np.ascontiguousarray(lb["v"][b, :, cs:cs + 256]),
            "w_i": np.ascontiguousarray(lb["w"][b, :, cs:cs + 256]),
            "u_i": np.ascontiguousarray(u[hq * 4:(hq + 1) * 4].reshape(1, 256)),
        })
    res = run_bass_kernel_spmd(nc, in_maps, core_ids=list(range(NCORES)))
    R = res.results
    y = np.zeros((2, T, 1024), np.float32)
    for c in range(NCORES):
        b, hq = c // 4, c % 4
        y[b, :, hq * 256:(hq + 1) * 256] = R[c]["y_o"]
    return y



def build_l0post():
    nc = bacc_mod.Bacc("TRN2", target_bir_lowering=False)
    y_d = nc.dram_tensor("y_i", [TPC, 1024], F32, kind="ExternalInput")
    g_d = nc.dram_tensor("g_i", [TPC, 1024], F32, kind="ExternalInput")
    xn_d = nc.dram_tensor("xn_i", [TPC, 1024], F32, kind="ExternalInput")
    x_d = nc.dram_tensor("x_i", [TPC, 1024], F32, kind="ExternalInput")
    rows_d = nc.dram_tensor("rows_i", [6, 1024], F32, kind="ExternalInput")
    WoT_d = nc.dram_tensor("WoT", [1024, 1024], F32, kind="ExternalInput")
    awT_d = nc.dram_tensor("awT", [1024, 1024], F32, kind="ExternalInput")
    g0_d = nc.dram_tensor("g0T", [1024, 8], F32, kind="ExternalInput")
    g1_d = nc.dram_tensor("g1T", [1024, 8], F32, kind="ExternalInput")
    gpT_d = nc.dram_tensor("gpT", [8, 1024, 4096], F32, kind="ExternalInput")
    upT_d = nc.dram_tensor("upT", [8, 1024, 4096], F32, kind="ExternalInput")
    dpT_d = nc.dram_tensor("dpT", [8, 4096, 1024], F32, kind="ExternalInput")
    x1_o = nc.dram_tensor("x1_o", [TPC, 1024], F32, kind="ExternalOutput")
    xn1_o = nc.dram_tensor("xn1_o", [TPC, 1024], F32, kind="ExternalOutput")
    xn1T_o = nc.dram_tensor("xn1T_o", [1024, TPC], F32, kind="ExternalOutput")
    sc_o = nc.dram_tensor("sc_o", [TPC, 8], F32, kind="ExternalOutput")
    sT = nc.dram_tensor("sT_scr", [1024, TPC], F32, kind="Internal")
    h_all = nc.dram_tensor("h_scr", [TPC, 1024], F32, kind="Internal")
    s_all = nc.dram_tensor("s_scr", [TPC, 1024], F32, kind="Internal")
    accT = nc.dram_tensor("accT_scr", [1024, TPC], F32, kind="Internal")
    comb_f = nc.dram_tensor("comb_scr", [1, 8 * TPC], F32, kind="Internal")

    with contextlib.ExitStack() as ctx:
        sb = lambda n, s, d=F32: ctx.enter_context(nc.sbuf_tensor(n, s, d))
        ps = lambda n, s, d=F32: ctx.enter_context(nc.psum_tensor(n, s, d))
        AR1 = sb("AR1", [128, 8192]); AR2 = sb("AR2", [128, 8192])
        AR3 = sb("AR3", [128, 8192])
        bc = sb("bcp", [128, 6 * 1024])
        y_t = sb("y_t", [128, 1024]); g_t = sb("g_t", [128, 1024])
        xn_t = sb("xn_t", [128, 1024]); tmp = sb("tmp1", [128, 1024])
        t_a = sb("t_a", [128, 1024]); t_b = sb("t_b", [128, 1024])
        tT = sb("tTl", [128, 1024])
        HeT = sb("HeT", [128, 8 * 512]); tTc = sb("tTc", [128, 8 * 512])
        accld = sb("accld", [128, 512]); combbc = sb("combbc", [128, 512])
        tmp512 = sb("tmp512", [128, 512])
        grow = sb("grow", [1, 512]); rows_sb = sb("rows_s2", [1, 1024])
        g0 = sb("g0s", [128, 64]); g1 = sb("g1s", [128, 64])
        st = sb("st2", [128, 8]); st16 = sb("st16", [128, 6 * 16])
        lg = sb("lg", [128, 8]); cmb = sb("cmb", [128, 8])
        c1 = sb("c1t", [128, 8]); c2 = sb("c2t", [128, 8])
        cm8 = sb("cm8", [8, 128])
        id_f = sb("id_f2", [128, 128])
        o_r = sb("o_r2", [1, 128]); o_c = sb("o_c2", [128, 1])
        pa = ps("pa2", [128, 512]); pb = ps("pb2", [128, 512])
        pbc = ps("pbc2", [128, 512]); p8 = ps("p82", [128, 8])
        ptf = ps("ptf2", [128, 128])
        sem = ctx.enter_context(nc.semaphore("chain"))
        p = P(nc, sem)
        mk_ident(p, id_f, o_r, o_c)
        for r in range(6):
            p.dma("sync", lambda o: rows_sb[:, :],
                  lambda o, r=r: bass.AP(rows_d, r * 1024, [[1024, 1], [1, 1024]]))
            bcast_row(p, lambda off, w: bass.AP(rows_sb, off, [[1024, 1], [1, w]]),
                      lambda off, w, r=r: bass.AP(bc, r * 1024 + off,
                                                  [[6144, 128], [1, w]]),
                      1024, o_r, pbc)

        def bca(r):
            return bass.AP(bc, r * 1024, [[6144, 128], [1, 1024]])
        for j in range(8):
            p.dma("sync", lambda o, j=j: bass.AP(AR1, j * 1024, [[8192, 128], [1, 1024]]),
                  lambda o, j=j: bass.AP(WoT_d, j * 128 * 1024, [[1024, 128], [1, 1024]]))
            p.dma("sync", lambda o, j=j: bass.AP(AR2, j * 1024, [[8192, 128], [1, 1024]]),
                  lambda o, j=j: bass.AP(awT_d, j * 128 * 1024, [[1024, 128], [1, 1024]]))
        for j in range(8):
            p.dma("sync", lambda o, j=j: bass.AP(g0, j * 8, [[64, 128], [1, 8]]),
                  lambda o, j=j: bass.AP(g0_d, j * 128 * 8, [[8, 128], [1, 8]]))
            p.dma("sync", lambda o, j=j: bass.AP(g1, j * 8, [[64, 128], [1, 8]]),
                  lambda o, j=j: bass.AP(g1_d, j * 128 * 8, [[8, 128], [1, 8]]))
        p.S("gpsimd", lambda g, o: g.memset(bass.AP(AR3, 0, [[8192, 128], [1, 4096]]), 0.0))
        p.loop(8, lambda i: p.dma("gpsimd",
               lambda o: bass.AP(accT, o([(i, 128 * TPC)]), [[TPC, 128], [1, 4096]]),
               lambda o: bass.AP(AR3, 0, [[8192, 128], [1, 4096]])))

        def p1(i):
            p.dma("sync", lambda o: y_t[:, :],
                  lambda o: bass.AP(y_d, o([(i, 128 * 1024)]), [[1024, 128], [1, 1024]]))
            p.dma("sync", lambda o: g_t[:, :],
                  lambda o: bass.AP(g_d, o([(i, 128 * 1024)]), [[1024, 128], [1, 1024]]))
            p.dma("sync", lambda o: xn_t[:, :],
                  lambda o: bass.AP(xn_d, o([(i, 128 * 1024)]), [[1024, 128], [1, 1024]]))
            y3 = bass.AP(y_t, 0, [[1024, 128], [64, 16], [1, 64]])
            p.S("vector", lambda v, o: v.reduce_sum(
                bass.AP(st16, 0, [[96, 128], [1, 16], [0, 1]]), y3, axis=AX.X))
            p.S("vector", lambda v, o: v.tensor_scalar_mul(
                st16[:, 16:32], st16[:, 0:16], 1.0 / 64))
            p.S("scalar", lambda s, o: s.activation(tmp[:, :], y_t[:, :], AF.Square))
            p.S("vector", lambda v, o: v.reduce_sum(
                bass.AP(st16, 32, [[96, 128], [1, 16], [0, 1]]),
                bass.AP(tmp, 0, [[1024, 128], [64, 16], [1, 64]]), axis=AX.X))
            p.S("vector", lambda v, o: v.tensor_scalar_mul(
                st16[:, 48:64], st16[:, 32:48], 1.0 / 64))
            p.S("scalar", lambda s, o: s.activation(st16[:, 64:80], st16[:, 16:32],
                                                    AF.Square))
            p.S("vector", lambda v, o: v.tensor_sub(st16[:, 48:64], st16[:, 48:64],
                                                    st16[:, 64:80]))
            p.S("vector", lambda v, o: v.tensor_scalar_add(st16[:, 48:64],
                                                           st16[:, 48:64], 1.6e-4))
            p.S("scalar", lambda s, o: s.activation(st16[:, 64:80], st16[:, 48:64], AF.Ln))
            p.S("scalar", lambda s, o: s.activation(st16[:, 48:64], st16[:, 64:80],
                                                    AF.Exp, scale=-0.5))
            p.S("vector", lambda v, o: v.tensor_mul(st16[:, 64:80], st16[:, 16:32],
                                                    st16[:, 48:64]))
            for hh in range(16):
                p.S("vector", lambda v, o, hh=hh: v.tensor_scalar(
                    y_t[:, hh * 64:(hh + 1) * 64], y_t[:, hh * 64:(hh + 1) * 64],
                    st16[:, 48 + hh:49 + hh], st16[:, 64 + hh:65 + hh],
                    ALU.mult, ALU.subtract))
            p.S("vector", lambda v, o: v.tensor_mul(y_t[:, :], y_t[:, :], bca(0)))
            p.S("vector", lambda v, o: v.tensor_add(y_t[:, :], y_t[:, :], bca(1)))
            p.S("vector", lambda v, o: v.tensor_mul(tmp[:, :], y_t[:, :], g_t[:, :]))
            transpose8(p, tmp, tT, id_f, ptf)
            for half, pp in ((0, pa), (1, pb)):
                mm_acc(p, pp[:, :], lambda j: tT[:, j * 128:(j + 1) * 128],
                       lambda j, half=half: bass.AP(AR1, j * 1024 + half * 512,
                                                    [[8192, 128], [1, 512]]), 8)
            for half, pp in ((0, pa), (1, pb)):
                p.S("vector", lambda v, o, half=half, pp=pp: v.tensor_copy(
                    tmp[:, half * 512:(half + 1) * 512], pp[:, :]))
            p.dma("gpsimd", lambda o: bass.AP(s_all, o([(i, 128 * 1024)]),
                                              [[1024, 128], [1, 1024]]),
                  lambda o: tmp[:, :])
            layernorm(p, xn_t, t_a, st, 1024, sq_sb=tmp,
                      gamma_bc=bca(2), beta_bc=bca(3))
            transpose8(p, t_a, tT, id_f, ptf)
            for half, pp in ((0, pa), (1, pb)):
                mm_acc(p, pp[:, :], lambda j: tT[:, j * 128:(j + 1) * 128],
                       lambda j, half=half: bass.AP(AR2, j * 1024 + half * 512,
                                                    [[8192, 128], [1, 512]]), 8)
            for half, pp in ((0, pa), (1, pb)):
                p.S("vector", lambda v, o, half=half, pp=pp: v.tensor_add(
                    t_a[:, half * 512:(half + 1) * 512], pp[:, :],
                    xn_t[:, half * 512:(half + 1) * 512]))
            p.dma("gpsimd", lambda o: bass.AP(h_all, o([(i, 128 * 1024)]),
                                              [[1024, 128], [1, 1024]]),
                  lambda o: t_a[:, :])
            layernorm(p, t_a, t_b, st, 1024, sq_sb=tmp,
                      gamma_bc=bca(4), beta_bc=bca(5))
            transpose8(p, t_b, tT, id_f, ptf)
            for j in range(8):
                p.dma("gpsimd",
                      lambda o, j=j: bass.AP(sT, o([(i, 128)], j * 128 * TPC),
                                             [[TPC, 128], [1, 128]]),
                      lambda o, j=j: tT[:, j * 128:(j + 1) * 128])
            mm_acc(p, p8[:, :], lambda j: tT[:, j * 128:(j + 1) * 128],
                   lambda j: bass.AP(g0, j * 8, [[64, 128], [1, 8]]), 8)
            p.S("vector", lambda v, o: v.tensor_copy(lg[:, :], p8[:, :]))
            p.S("vector", lambda v, o: v.reduce_max(st[:, 0:1], lg[:, :], axis=AX.X))
            p.S("vector", lambda v, o: v.tensor_scalar(c1[:, :], lg[:, :],
                                                       st[:, 0:1], None, ALU.is_ge))
            p.S("vector", lambda v, o: v.scalar_tensor_tensor(
                cmb[:, :], c1[:, :], -1e9, lg[:, :], ALU.mult, ALU.add))
            p.S("vector", lambda v, o: v.reduce_max(st[:, 1:2], cmb[:, :], axis=AX.X))
            p.S("vector", lambda v, o: v.tensor_sub(st[:, 2:3], st[:, 0:1], st[:, 1:2]))
            p.S("scalar", lambda s, o: s.activation(st[:, 3:4], st[:, 2:3], AF.Sigmoid))
            p.S("scalar", lambda s, o: s.activation(st[:, 4:5], st[:, 2:3], AF.Sigmoid,
                                                    scale=-1.0))
            p.S("vector", lambda v, o: v.tensor_scalar(c2[:, :], lg[:, :],
                                                       st[:, 1:2], None, ALU.is_ge))
            p.S("vector", lambda v, o: v.tensor_sub(c2[:, :], c2[:, :], c1[:, :]))
            p.S("vector", lambda v, o: v.tensor_scalar(c1[:, :], c1[:, :],
                                                       st[:, 3:4], None, ALU.mult))
            p.S("vector", lambda v, o: v.tensor_scalar(c2[:, :], c2[:, :],
                                                       st[:, 4:5], None, ALU.mult))
            p.S("vector", lambda v, o: v.tensor_add(cmb[:, :], c1[:, :], c2[:, :]))
            p.S("tensor", lambda t, o: t.transpose(
                bass.AP(ptf, 0, [[128, 8], [1, 128]]), cmb[:, :], id_f[:, :]))
            p.S("vector", lambda v, o: v.tensor_copy(
                cm8[:, :], bass.AP(ptf, 0, [[128, 8], [1, 128]])))
            p.dma("gpsimd", lambda o: bass.AP(comb_f, o([(i, 128)]),
                                              [[8 * TPC, 1], [TPC, 8], [1, 128]]),
                  lambda o: cm8[:, :])
        p.loop(TILES, p1)

        def emoe(e):
            def fqb(fq):
                for j in range(8):
                    p.dma("sync",
                          lambda o, j=j: bass.AP(AR1, j * 1024, [[8192, 128], [1, 1024]]),
                          lambda o, j=j: bass.AP(
                              gpT_d, o([(e, 1024 * 4096), (fq, 1024)], j * 128 * 4096),
                              [[4096, 128], [1, 1024]]))
                    p.dma("sync",
                          lambda o, j=j: bass.AP(AR2, j * 1024, [[8192, 128], [1, 1024]]),
                          lambda o, j=j: bass.AP(
                              upT_d, o([(e, 1024 * 4096), (fq, 1024)], j * 128 * 4096),
                              [[4096, 128], [1, 1024]]))
                    p.dma("sync",
                          lambda o, j=j: bass.AP(AR3, j * 1024, [[8192, 128], [1, 1024]]),
                          lambda o, j=j: bass.AP(
                              dpT_d, o([(e, 4096 * 1024), (fq, 1024 * 1024)],
                                       j * 128 * 1024),
                              [[1024, 128], [1, 1024]]))

                def tcb(tc):
                    for j in range(8):
                        p.dma("sync",
                              lambda o, j=j: bass.AP(tTc, j * 512, [[4096, 128], [1, 512]]),
                              lambda o, j=j: bass.AP(sT, o([(tc, 512)], j * 128 * TPC),
                                                     [[TPC, 128], [1, 512]]))
                    p.dma("sync", lambda o: grow[:, :],
                          lambda o: bass.AP(comb_f, o([(e, TPC), (tc, 512)]),
                                            [[8 * TPC, 1], [1, 512]]))
                    p.S("tensor", lambda t, o: t.matmul(pbc[:, :], o_r[0:1, :],
                                                        grow[0:1, :],
                                                        start=True, stop=True))
                    p.S("vector", lambda v, o: v.tensor_copy(combbc[:, :], pbc[:, :]))
                    for fc in range(8):
                        mm_acc(p, pa[:, :],
                               lambda j, fc=fc: bass.AP(
                                   AR1, j * 1024 + fc * 128, [[8192, 128], [1, 128]]),
                               lambda j: bass.AP(tTc, j * 512, [[4096, 128], [1, 512]]), 8)
                        mm_acc(p, pb[:, :],
                               lambda j, fc=fc: bass.AP(
                                   AR2, j * 1024 + fc * 128, [[8192, 128], [1, 128]]),
                               lambda j: bass.AP(tTc, j * 512, [[4096, 128], [1, 512]]), 8)
                        p.S("scalar", lambda s, o: s.activation(tmp512[:, :], pa[:, :],
                                                                AF.Silu))
                        p.S("vector", lambda v, o, fc=fc: v.tensor_mul(
                            HeT[:, fc * 512:(fc + 1) * 512], tmp512[:, :], pb[:, :]))
                    for dc in range(8):
                        mm_acc(p, pa[:, :],
                               lambda fc, dc=dc: bass.AP(
                                   AR3, fc * 1024 + dc * 128, [[8192, 128], [1, 128]]),
                               lambda fc: HeT[:, fc * 512:(fc + 1) * 512], 8)
                        p.dma("sync", lambda o: accld[:, :],
                              lambda o, dc=dc: bass.AP(
                                  accT, o([(tc, 512)], dc * 128 * TPC),
                                  [[TPC, 128], [1, 512]]))
                        p.S("vector", lambda v, o: v.tensor_mul(tmp512[:, :], pa[:, :],
                                                                combbc[:, :]))
                        p.S("vector", lambda v, o: v.tensor_add(accld[:, :], accld[:, :],
                                                                tmp512[:, :]))
                        p.dma("gpsimd",
                              lambda o, dc=dc: bass.AP(
                                  accT, o([(tc, 512)], dc * 128 * TPC),
                                  [[TPC, 128], [1, 512]]),
                              lambda o: accld[:, :])
                p.loop(8, tcb)
            p.loop(4, fqb)
        p.loop(8, emoe)

        def p3(i):
            for j in range(8):
                p.dma("sync", lambda o, j=j: accld[:, 0:128],
                      lambda o, j=j: bass.AP(accT, o([(i, 128)], j * 128 * TPC),
                                             [[TPC, 128], [1, 128]]))
                p.S("tensor", lambda t, o: t.transpose(ptf[:, :], accld[:, 0:128],
                                                       id_f[:, :]))
                p.S("vector", lambda v, o, j=j: v.tensor_copy(
                    t_b[:, j * 128:(j + 1) * 128], ptf[:, :]))
            p.dma("sync", lambda o: t_a[:, :],
                  lambda o: bass.AP(h_all, o([(i, 128 * 1024)]), [[1024, 128], [1, 1024]]))
            p.dma("sync", lambda o: y_t[:, :],
                  lambda o: bass.AP(s_all, o([(i, 128 * 1024)]), [[1024, 128], [1, 1024]]))
            p.dma("sync", lambda o: g_t[:, :],
                  lambda o: bass.AP(x_d, o([(i, 128 * 1024)]), [[1024, 128], [1, 1024]]))
            p.S("vector", lambda v, o: v.tensor_add(t_b[:, :], t_b[:, :], t_a[:, :]))
            p.S("scalar", lambda s, o: s.activation(t_b[:, :], t_b[:, :], AF.Silu))
            p.S("vector", lambda v, o: v.tensor_mul(t_b[:, :], t_b[:, :], y_t[:, :]))
            p.S("vector", lambda v, o: v.tensor_add(t_b[:, :], t_b[:, :], g_t[:, :]))
            p.dma("gpsimd", lambda o: bass.AP(x1_o, o([(i, 128 * 1024)]),
                                              [[1024, 128], [1, 1024]]),
                  lambda o: t_b[:, :])
            layernorm(p, t_b, xn_t, st, 1024, sq_sb=tmp)
            p.dma("gpsimd", lambda o: bass.AP(xn1_o, o([(i, 128 * 1024)]),
                                              [[1024, 128], [1, 1024]]),
                  lambda o: xn_t[:, :])
            transpose8(p, xn_t, tT, id_f, ptf)
            for j in range(8):
                p.dma("gpsimd",
                      lambda o, j=j: bass.AP(xn1T_o, o([(i, 128)], j * 128 * TPC),
                                             [[TPC, 128], [1, 128]]),
                      lambda o, j=j: tT[:, j * 128:(j + 1) * 128])
            mm_acc(p, p8[:, :], lambda j: tT[:, j * 128:(j + 1) * 128],
                   lambda j: bass.AP(g1, j * 8, [[64, 128], [1, 8]]), 8)
            p.S("vector", lambda v, o: v.reduce_max(st[:, 0:1], p8[:, :], axis=AX.X))
            p.S("vector", lambda v, o: v.tensor_scalar_mul(st[:, 1:2], st[:, 0:1], -1.0))
            p.S("scalar", lambda s, o: s.activation(lg[:, :], p8[:, :], AF.Exp,
                                                    bias=st[:, 1:2]))
            p.S("vector", lambda v, o: v.reduce_sum(st[:, 2:3], lg[:, :], axis=AX.X))
            p.S("vector", lambda v, o: v.reciprocal(st[:, 3:4], st[:, 2:3]))
            p.S("vector", lambda v, o: v.tensor_scalar(lg[:, :], lg[:, :],
                                                       st[:, 3:4], None, ALU.mult))
            p.dma("gpsimd", lambda o: bass.AP(sc_o, o([(i, 128 * 8)]),
                                              [[8, 128], [1, 8]]),
                  lambda o: lg[:, :])
        p.loop(TILES, p3)
        p.final_wait("gpsimd")
    nc.finalize()
    return nc


def run_l0post(lb, yv, X):
    nc = build_l0post()
    rows = np.stack([X["L0_attn_gn_gamma"], X["L0_attn_gn_beta"],
                     X["L0_ffn_ln1_g"], X["L0_ffn_ln1_b"],
                     X["L0_ffn_ln2_g"], X["L0_ffn_ln2_b"]])
    common = {
        "rows_i": np.ascontiguousarray(rows),
        "WoT": np.ascontiguousarray(X["L0_attn_Wo"].T),
        "awT": np.ascontiguousarray(X["L0_ffn_attn_w"].T),
        "g0T": np.ascontiguousarray(X["L0_ffn_gate_w"].T),
        "g1T": np.ascontiguousarray(X["L1_ffn_gate_weight"].T),
        "gpT": np.ascontiguousarray(np.transpose(X["L0_ffn_gate_proj"], (0, 2, 1))),
        "upT": np.ascontiguousarray(np.transpose(X["L0_ffn_up_proj"], (0, 2, 1))),
        "dpT": np.ascontiguousarray(np.transpose(X["L0_ffn_down_proj"], (0, 2, 1))),
    }
    x = X["L0_attn_x"]
    in_maps = []
    for c in range(NCORES):
        b, q = c // 4, c % 4
        sl = slice(q * TPC, (q + 1) * TPC)
        m = dict(common)
        m["y_i"] = np.ascontiguousarray(yv[b, sl])
        m["g_i"] = np.ascontiguousarray(lb["g"][b, sl])
        m["xn_i"] = np.ascontiguousarray(lb["xn"][b, sl])
        m["x_i"] = np.ascontiguousarray(x[b, sl])
        in_maps.append(m)
    res = run_bass_kernel_spmd(nc, in_maps, core_ids=list(range(NCORES)))
    R = res.results
    out = {}
    for key, nm in (("x1", "x1_o"), ("xn1", "xn1_o")):
        out[key] = np.stack([np.stack([R[b * 4 + q][nm] for q in range(4)])
                             for b in range(2)]).reshape(2, 16384, 1024)
    out["scores"] = np.concatenate([R[c]["sc_o"] for c in range(NCORES)], axis=0)
    out["xn1T"] = [np.concatenate([R[b * 4 + q]["xn1T_o"] for q in range(4)], axis=1)
                   for b in range(2)]
    return out



def build_l1attn():
    nc = bacc_mod.Bacc("TRN2", target_bir_lowering=False)
    kT_d = nc.dram_tensor("kT_i", [1024, T], F32, kind="ExternalInput")
    vn_d = nc.dram_tensor("vn_i", [T, 1024], F32, kind="ExternalInput")
    W_d = nc.dram_tensor("W_i", [1024, 1024], F32, kind="ExternalInput")
    qT_my = nc.dram_tensor("qTm_i", [1024, TPC], F32, kind="ExternalInput")
    s1_o = nc.dram_tensor("s1_o", [TPC, 1024], F32, kind="ExternalOutput")
    NKB = T // 1024

    with contextlib.ExitStack() as ctx:
        sb = lambda n, s, d=F32: ctx.enter_context(nc.sbuf_tensor(n, s, d))
        ps = lambda n, s, d=F32: ctx.enter_context(nc.psum_tensor(n, s, d))
        Kb = sb("Kb", [128, 8 * 1024]); Vb = sb("Vb", [128, 8 * 1024])
        qT = sb("qT", [128, 8 * 1024]); att = sb("att", [128, 8 * 1024])
        wsum = sb("wsum", [128, 1024]); wrow = sb("wrow", [1, 1024])
        w8 = sb("w8", [128, 8]); wr8 = sb("wr8", [8, 128])
        cs = sb("cs", [128, 512]); e_t = sb("e_t", [128, 512])
        ve = sb("ve", [128, 512]); pk = sb("pk", [128, 512])
        pkT = sb("pkT", [128, 512])
        m_t = sb("m_t", [128, 8]); l_t = sb("l_t", [128, 8])
        vac = sb("vac", [128, 8]); ks = sb("ks", [128, 8])
        nm = sb("nm", [128, 8]); c01 = sb("c01", [128, 8])
        st = sb("st3", [128, 8]); s1_t = sb("s1_t", [128, 1024])
        id_f = sb("id_f3", [128, 128])
        o_r = sb("o_r3", [1, 128]); o_c = sb("o_c3", [128, 1])
        pscore = ps("pscore", [128, 512])
        patt0 = ps("patt0", [128, 512]); patt1 = ps("patt1", [128, 512])
        ptf = ps("ptf3", [128, 128]); pbc = ps("pbc3", [128, 512])
        sem = ctx.enter_context(nc.semaphore("chain"))
        p = P(nc, sem)
        mk_ident(p, id_f, o_r, o_c)
        for j in range(8):
            p.dma("sync", lambda o: s1_t[:, :],
                  lambda o, j=j: bass.AP(W_d, j * 128 * 1024, [[1024, 128], [1, 1024]]))
            p.S("vector", lambda v, o, j=j: v.reduce_sum(w8[:, j:j + 1], s1_t[:, :],
                                                         axis=AX.X))
        p.S("tensor", lambda t, o: t.transpose(
            bass.AP(ptf, 0, [[128, 8], [1, 128]]), w8[:, :], id_f[:, :]))
        p.S("vector", lambda v, o: v.tensor_copy(
            wr8[:, :], bass.AP(ptf, 0, [[128, 8], [1, 128]])))
        p.dma("gpsimd", lambda o: bass.AP(wrow, 0, [[1024, 1], [128, 8], [1, 128]]),
              lambda o: bass.AP(wr8, 0, [[128, 8], [1, 128]]))
        for half in range(2):
            p.S("tensor", lambda t, o, half=half: t.matmul(
                pbc[:, :], o_r[0:1, :], wrow[0:1, half * 512:(half + 1) * 512],
                start=True, stop=True))
            p.S("vector", lambda v, o, half=half: v.tensor_copy(
                wsum[:, half * 512:(half + 1) * 512], pbc[:, :]))

        def qgb(qg):
            def qload(qt):
                p.dma("sync", lambda o: bass.AP(qT, o([(qt, 1024)]),
                                                [[8192, 128], [128, 8], [1, 128]]),
                      lambda o: bass.AP(qT_my, o([(qt, 128), (qg, 1024)]),
                                        [[TPC, 128], [128 * TPC, 8], [1, 128]]))
            p.loop(8, qload)
            p.S("gpsimd", lambda g, o: g.memset(m_t[:, :], -1e30))
            p.S("gpsimd", lambda g, o: g.memset(l_t[:, :], 0.0))
            p.S("gpsimd", lambda g, o: g.memset(vac[:, :], 0.0))
            p.S("gpsimd", lambda g, o: g.memset(ks[:, :], 0.0))
            p.S("gpsimd", lambda g, o: g.memset(att[:, :], 0.0))

            def p1body(kb):
                p.dma("sync", lambda o: Kb[:, :],
                      lambda o: bass.AP(kT_d, o([(kb, 1024)]),
                                        [[T, 128], [128 * T, 8], [1, 1024]]))
                for qt in range(8):
                    for c2 in range(2):
                        def emit_mm(t, o, qt=qt, c2=c2):
                            out = []
                            for j in range(8):
                                out.append(t.matmul(
                                    pscore[:, :],
                                    qT[:, qt * 1024 + j * 128:qt * 1024 + j * 128 + 128],
                                    Kb[:, j * 1024 + c2 * 512:j * 1024 + c2 * 512 + 512],
                                    start=(j == 0), stop=(j == 7)))
                            return out
                        p.S("tensor", emit_mm)
                        p.S("scalar", lambda s, o: s.activation(cs[:, :], pscore[:, :],
                                                                AF.Copy, scale=0.03125))
                        p.S("vector", lambda v, o: v.reduce_max(st[:, 0:1], cs[:, :],
                                                                axis=AX.X))
                        p.S("vector", lambda v, o, qt=qt: v.tensor_tensor(
                            st[:, 1:2], st[:, 0:1], m_t[:, qt:qt + 1], ALU.max))
                        p.S("vector", lambda v, o, qt=qt: v.tensor_sub(
                            st[:, 2:3], m_t[:, qt:qt + 1], st[:, 1:2]))
                        p.S("scalar", lambda s, o: s.activation(st[:, 3:4], st[:, 2:3],
                                                                AF.Exp))
                        p.S("vector", lambda v, o: v.tensor_scalar_mul(
                            st[:, 4:5], st[:, 1:2], -1.0))
                        p.S("scalar", lambda s, o: s.activation(e_t[:, :], cs[:, :],
                                                                AF.Exp, bias=st[:, 4:5]))
                        p.S("vector", lambda v, o: v.reduce_sum(st[:, 5:6], e_t[:, :],
                                                                axis=AX.X))
                        p.S("vector", lambda v, o, qt=qt: v.scalar_tensor_tensor(
                            l_t[:, qt:qt + 1], l_t[:, qt:qt + 1],
                            st[:, 3:4], st[:, 5:6], ALU.mult, ALU.add))
                        p.S("vector", lambda v, o, qt=qt: v.tensor_copy(
                            m_t[:, qt:qt + 1], st[:, 1:2]))
            p.loop(NKB, p1body)

            def midb(qt):
                p.S("vector", lambda v, o, qt=qt: v.tensor_scalar_mul(
                    nm[:, qt:qt + 1], m_t[:, qt:qt + 1], -1.0))
                p.S("vector", lambda v, o, qt=qt: v.tensor_scalar_mul(
                    c01[:, qt:qt + 1], l_t[:, qt:qt + 1], 0.01))
            for qt in range(8):
                midb(qt)

            def p2body(kb):
                p.dma("sync", lambda o: Kb[:, :],
                      lambda o: bass.AP(kT_d, o([(kb, 1024)]),
                                        [[T, 128], [128 * T, 8], [1, 1024]]))
                p.dma("sync", lambda o: Vb[:, :],
                      lambda o: bass.AP(vn_d, o([(kb, 1024 * 1024)]),
                                        [[1024, 128], [128 * 1024, 8], [1, 1024]]))
                for qt in range(8):
                    for c2 in range(2):
                        def emit_mm(t, o, qt=qt, c2=c2):
                            out = []
                            for j in range(8):
                                out.append(t.matmul(
                                    pscore[:, :],
                                    qT[:, qt * 1024 + j * 128:qt * 1024 + j * 128 + 128],
                                    Kb[:, j * 1024 + c2 * 512:j * 1024 + c2 * 512 + 512],
                                    start=(j == 0), stop=(j == 7)))
                            return out
                        p.S("tensor", emit_mm)
                        p.S("scalar", lambda s, o, qt=qt: s.activation(
                            e_t[:, :], pscore[:, :], AF.Exp,
                            bias=nm[:, qt:qt + 1], scale=0.03125))
                        p.S("vector", lambda v, o, qt=qt: v.tensor_scalar(
                            ve[:, :], e_t[:, :], c01[:, qt:qt + 1], None, ALU.is_lt))
                        p.S("vector", lambda v, o: v.tensor_mul(ve[:, :], ve[:, :],
                                                                e_t[:, :]))
                        p.S("vector", lambda v, o: v.reduce_sum(st[:, 0:1], ve[:, :],
                                                                axis=AX.X))
                        p.S("vector", lambda v, o, qt=qt: v.tensor_add(
                            vac[:, qt:qt + 1], vac[:, qt:qt + 1], st[:, 0:1]))
                        p.S("vector", lambda v, o: v.tensor_sub(pk[:, :], e_t[:, :],
                                                                ve[:, :]))
                        p.S("vector", lambda v, o: v.reduce_sum(st[:, 1:2], pk[:, :],
                                                                axis=AX.X))
                        p.S("vector", lambda v, o, qt=qt: v.tensor_add(
                            ks[:, qt:qt + 1], ks[:, qt:qt + 1], st[:, 1:2]))
                        for j4 in range(4):
                            p.S("tensor", lambda t, o, j4=j4: t.transpose(
                                ptf[:, :], pk[:, j4 * 128:(j4 + 1) * 128], id_f[:, :]))
                            p.S("vector", lambda v, o, j4=j4: v.tensor_copy(
                                pkT[:, j4 * 128:(j4 + 1) * 128], ptf[:, :]))

                        def emit_pv(t, o, c2=c2):
                            out = []
                            for j4 in range(4):
                                ko = c2 * 4 + j4
                                for half, pp in ((0, patt0), (1, patt1)):
                                    out.append(t.matmul(
                                        pp[:, :], pkT[:, j4 * 128:(j4 + 1) * 128],
                                        Vb[:, ko * 1024 + half * 512:ko * 1024 + half * 512 + 512],
                                        start=(c2 == 0 and j4 == 0),
                                        stop=(c2 == 1 and j4 == 3)))
                            return out
                        p.S("tensor", emit_pv)
                    for half, pp in ((0, patt0), (1, patt1)):
                        p.S("vector", lambda v, o, qt=qt, half=half, pp=pp: v.tensor_add(
                            att[:, qt * 1024 + half * 512:qt * 1024 + half * 512 + 512],
                            att[:, qt * 1024 + half * 512:qt * 1024 + half * 512 + 512],
                            pp[:, :]))
            p.loop(NKB, p2body)

            def finb(qt):
                p.S("vector", lambda v, o, qt=qt: v.tensor_scalar(
                    st[:, 0:1], ks[:, qt:qt + 1], 1e-9, None, ALU.add))
                p.S("vector", lambda v, o: v.reciprocal(st[:, 1:2], st[:, 0:1]))
                p.S("vector", lambda v, o, qt=qt: v.reciprocal(
                    st[:, 3:4], l_t[:, qt:qt + 1]))
                p.S("vector", lambda v, o, qt=qt: v.tensor_mul(
                    st[:, 2:3], vac[:, qt:qt + 1], st[:, 3:4]))
                p.S("vector", lambda v, o: v.tensor_scalar_mul(
                    st[:, 2:3], st[:, 2:3], 0.01))
                p.S("vector", lambda v, o, qt=qt: v.tensor_scalar(
                    s1_t[:, :], att[:, qt * 1024:(qt + 1) * 1024],
                    st[:, 1:2], None, ALU.mult))
                p.S("vector", lambda v, o: v.scalar_tensor_tensor(
                    s1_t[:, :], wsum[:, :], st[:, 2:3], s1_t[:, :],
                    ALU.mult, ALU.mult if False else ALU.add))
                p.dma("gpsimd",
                      lambda o, qt=qt: bass.AP(s1_o, o([(qt, 128 * 1024), (qg, 8 * 128 * 1024)]),
                                               [[1024, 128], [1, 1024]]),
                      lambda o: s1_t[:, :])
            for qt in range(8):
                finb(qt)
        p.loop(4, qgb)
        p.final_wait("gpsimd")
    nc.finalize()
    return nc


def run_l1attn(ld, X):
    nc = build_l1attn()
    in_maps = []
    for c in range(NCORES):
        b = c // 4
        q = c % 4
        in_maps.append({
            "kT_i": np.ascontiguousarray(ld["xn1T"][b]),
            "vn_i": np.ascontiguousarray(ld["xn1"][b]),
            "W_i": X["L1_attn_W"],
            "qTm_i": np.ascontiguousarray(ld["xn1T"][b][:, q * TPC:(q + 1) * TPC]),
        })
    res = run_bass_kernel_spmd(nc, in_maps, core_ids=list(range(NCORES)))
    R = res.results
    s1 = np.stack([np.stack([R[b * 4 + q]["s1_o"] for q in range(4)])
                   for b in range(2)]).reshape(2, 16384, 1024)
    return s1



def build_l1ffn():
    nc = bacc_mod.Bacc("TRN2", target_bir_lowering=False)
    xT_d = nc.dram_tensor("xT_i", [1024, TPC], F32, kind="ExternalInput")
    scf_d = nc.dram_tensor("scf_i", [NT, 8], F32, kind="ExternalInput")
    scm_d = nc.dram_tensor("scm_i", [TPC, 8], F32, kind="ExternalInput")
    s1_d = nc.dram_tensor("s1_i", [TPC, 1024], F32, kind="ExternalInput")
    x1_d = nc.dram_tensor("x1_i", [TPC, 1024], F32, kind="ExternalInput")
    W1_d = nc.dram_tensor("W1_i", [8, 1024, 4096], F32, kind="ExternalInput")
    b1_d = nc.dram_tensor("b1_i", [8, 4096], F32, kind="ExternalInput")
    W2_d = nc.dram_tensor("W2_i", [8, 4096, 1024], F32, kind="ExternalInput")
    b2_d = nc.dram_tensor("b2_i", [8, 1024], F32, kind="ExternalInput")
    Ws1_d = nc.dram_tensor("Ws1_i", [1024, 4096], F32, kind="ExternalInput")
    bs1_d = nc.dram_tensor("bs1_i", [1, 4096], F32, kind="ExternalInput")
    Ws2_d = nc.dram_tensor("Ws2_i", [4096, 1024], F32, kind="ExternalInput")
    bs2_d = nc.dram_tensor("bs2_i", [1, 1024], F32, kind="ExternalInput")
    out_d = nc.dram_tensor("out_o", [TPC, 1024], F32, kind="ExternalOutput")
    accT2 = nc.dram_tensor("accT2_scr", [1024, TPC], F32, kind="Internal")
    gate_f = nc.dram_tensor("gate_scr", [1, 8 * TPC], F32, kind="Internal")

    with contextlib.ExitStack() as ctx:
        sb = lambda n, s, d=F32: ctx.enter_context(nc.sbuf_tensor(n, s, d))
        ps = lambda n, s, d=F32: ctx.enter_context(nc.psum_tensor(n, s, d))
        AR1 = sb("AR1f", [128, 8192]); AR2 = sb("AR2f", [128, 8192])
        HeT = sb("HeTf", [128, 8 * 512]); tTc = sb("tTcf", [128, 8 * 512])
        sc_sb = sb("sc_sb", [128, 2048]); cmt = sb("cmt", [128, 2048])
        accld = sb("accldf", [128, 512]); combbc = sb("combbcf", [128, 512])
        t1_5 = sb("t1_5", [128, 512]); t2_5 = sb("t2_5", [128, 512])
        t_a = sb("t_af", [128, 1024]); t_b = sb("t_bf", [128, 1024])
        t_c = sb("t_cf", [128, 1024])
        grow = sb("growf", [1, 512])
        b1c = sb("b1c", [128, 8]); b2c = sb("b2c", [128, 8])
        b2cq = sb("b2cq", [128, 8])
        ls = sb("ls", [1, 8]); ls2 = sb("ls2", [1, 8])
        mb = sb("mb", [128, 1]); taub = sb("taub", [128, 1])
        ctv = sb("ctv", [128, 1])
        sct = sb("sct", [128, 8]); ktile = sb("ktile", [128, 8])
        gt8 = sb("gt8", [8, 128])
        id_f = sb("id_f4", [128, 128])
        o_r = sb("o_r4", [1, 128]); o_c = sb("o_c4", [128, 1])
        pa = ps("pa4", [128, 512]); pbc = ps("pbc4", [128, 512])
        p1c = ps("p1c4", [128, 1]); ptf = ps("ptf4", [128, 128])
        sem = ctx.enter_context(nc.semaphore("chain"))
        p = P(nc, sem)
        mk_ident(p, id_f, o_r, o_c)
        p.dma("sync", lambda o: sc_sb[:, :],
              lambda o: bass.AP(scf_d, 0, [[2048, 128], [1, 2048]]))
        p.S("gpsimd", lambda g, o: g.memset(ls[:, 0:1], 0.0))
        p.S("gpsimd", lambda g, o: g.memset(ls[:, 1:2], 1.0))

        def bis(it):
            p.S("vector", lambda v, o: v.tensor_add(ls[:, 2:3], ls[:, 0:1], ls[:, 1:2]))
            p.S("vector", lambda v, o: v.tensor_scalar_mul(ls[:, 2:3], ls[:, 2:3], 0.5))
            p.S("tensor", lambda t, o: t.matmul(p1c[:, :], o_r[0:1, :], ls[0:1, 2:3],
                                                start=True, stop=True))
            p.S("vector", lambda v, o: v.tensor_copy(mb[:, :], p1c[:, :]))
            p.S("vector", lambda v, o: v.tensor_scalar(cmt[:, :], sc_sb[:, :],
                                                       mb[:, 0:1], None, ALU.is_gt))
            p.S("vector", lambda v, o: v.reduce_sum(ctv[:, :], cmt[:, :], axis=AX.X))
            p.S("tensor", lambda t, o: t.matmul(p1c[0:1, 0:1], o_c[:, 0:1], ctv[:, 0:1],
                                                start=True, stop=True))
            p.S("vector", lambda v, o: v.tensor_copy(ls[:, 3:4], p1c[0:1, 0:1]))
            p.S("vector", lambda v, o: v.tensor_scalar(ls2[:, 0:1], ls[:, 3:4],
                                                       65536.0, None, ALU.is_ge))
            p.S("vector", lambda v, o: v.tensor_sub(ls2[:, 1:2], ls[:, 2:3], ls[:, 0:1]))
            p.S("vector", lambda v, o: v.tensor_mul(ls2[:, 2:3], ls2[:, 0:1], ls2[:, 1:2]))
            p.S("vector", lambda v, o: v.tensor_add(ls[:, 0:1], ls[:, 0:1], ls2[:, 2:3]))
            p.S("vector", lambda v, o: v.tensor_scalar(ls2[:, 3:4], ls2[:, 0:1],
                                                       -1.0, 1.0, ALU.mult, ALU.add))
            p.S("vector", lambda v, o: v.tensor_sub(ls2[:, 4:5], ls[:, 1:2], ls[:, 2:3]))
            p.S("vector", lambda v, o: v.tensor_mul(ls2[:, 5:6], ls2[:, 3:4], ls2[:, 4:5]))
            p.S("vector", lambda v, o: v.tensor_sub(ls[:, 1:2], ls[:, 1:2], ls2[:, 5:6]))
        p.loop(44, bis)
        p.S("tensor", lambda t, o: t.matmul(p1c[:, :], o_r[0:1, :], ls[0:1, 1:2],
                                            start=True, stop=True))
        p.S("vector", lambda v, o: v.tensor_copy(taub[:, :], p1c[:, :]))

        def gatb(i):
            p.dma("sync", lambda o: sct[:, :],
                  lambda o: bass.AP(scm_d, o([(i, 128 * 8)]), [[8, 128], [1, 8]]))
            p.S("vector", lambda v, o: v.tensor_scalar(ktile[:, :], sct[:, :],
                                                       taub[:, 0:1], None, ALU.is_ge))
            p.S("vector", lambda v, o: v.tensor_mul(ktile[:, :], ktile[:, :], sct[:, :]))
            p.S("tensor", lambda t, o: t.transpose(
                bass.AP(ptf, 0, [[128, 8], [1, 128]]), ktile[:, :], id_f[:, :]))
            p.S("vector", lambda v, o: v.tensor_copy(
                gt8[:, :], bass.AP(ptf, 0, [[128, 8], [1, 128]])))
            p.dma("gpsimd", lambda o: bass.AP(gate_f, o([(i, 128)]),
                                              [[8 * TPC, 1], [TPC, 8], [1, 128]]),
                  lambda o: gt8[:, :])
        p.loop(TILES, gatb)
        p.S("gpsimd", lambda g, o: g.memset(bass.AP(AR1, 0, [[8192, 128], [1, 4096]]), 0.0))
        p.loop(8, lambda i: p.dma("gpsimd",
               lambda o: bass.AP(accT2, o([(i, 128 * TPC)]), [[TPC, 128], [1, 4096]]),
               lambda o: bass.AP(AR1, 0, [[8192, 128], [1, 4096]])))

        def expert_block(use_gate, W1s, b1s, W2s, b2s, est, ecoef):
            def fqb(fq):
                for j in range(8):
                    p.dma("sync",
                          lambda o, j=j: bass.AP(AR1, j * 1024, [[8192, 128], [1, 1024]]),
                          lambda o, j=j: bass.AP(
                              W1s, o([(est, ecoef[0]), (fq, 1024)], j * 128 * 4096),
                              [[4096, 128], [1, 1024]]))
                for fc in range(8):
                    p.dma("sync",
                          lambda o, fc=fc: bass.AP(AR2, fc * 1024, [[8192, 128], [1, 1024]]),
                          lambda o, fc=fc: bass.AP(
                              W2s, o([(est, ecoef[1]), (fq, 1024 * 1024)],
                                     fc * 128 * 1024),
                              [[1024, 128], [1, 1024]]))
                for fc in range(8):
                    p.dma("sync", lambda o, fc=fc: b1c[:, fc:fc + 1],
                          lambda o, fc=fc: bass.AP(
                              b1s, o([(est, ecoef[2]), (fq, 1024)], fc * 128),
                              [[1, 128], [1, 1]]))
                for dc in range(8):
                    p.dma("sync", lambda o, dc=dc: b2c[:, dc:dc + 1],
                          lambda o, dc=dc: bass.AP(
                              b2s, o([(est, ecoef[3])], dc * 128),
                              [[1, 128], [1, 1]]))
                p.S("vector", lambda v, o: v.tensor_scalar_mul(b2cq[:, :], b2c[:, :], 0.25))

                def tcb(tc):
                    for j in range(8):
                        p.dma("sync",
                              lambda o, j=j: bass.AP(tTc, j * 512, [[4096, 128], [1, 512]]),
                              lambda o, j=j: bass.AP(xT_d, o([(tc, 512)], j * 128 * TPC),
                                                     [[TPC, 128], [1, 512]]))
                    if use_gate:
                        p.dma("sync", lambda o: grow[:, :],
                              lambda o: bass.AP(gate_f, o([(est, TPC), (tc, 512)]),
                                                [[8 * TPC, 1], [1, 512]]))
                        p.S("tensor", lambda t, o: t.matmul(pbc[:, :], o_r[0:1, :],
                                                            grow[0:1, :],
                                                            start=True, stop=True))
                        p.S("vector", lambda v, o: v.tensor_copy(combbc[:, :], pbc[:, :]))
                    for fc in range(8):
                        mm_acc(p, pa[:, :],
                               lambda j, fc=fc: bass.AP(
                                   AR1, j * 1024 + fc * 128, [[8192, 128], [1, 128]]),
                               lambda j: bass.AP(tTc, j * 512, [[4096, 128], [1, 512]]), 8)
                        p.S("vector", lambda v, o, fc=fc: v.tensor_scalar(
                            t1_5[:, :], pa[:, :], b1c[:, fc:fc + 1], None, ALU.add))
                        p.S("scalar", lambda s, o: s.activation(t2_5[:, :], t1_5[:, :],
                                                                AF.Square))
                        p.S("vector", lambda v, o: v.tensor_mul(t2_5[:, :], t2_5[:, :],
                                                                t1_5[:, :]))
                        p.S("vector", lambda v, o: v.scalar_tensor_tensor(
                            t2_5[:, :], t2_5[:, :], 0.044715, t1_5[:, :],
                            ALU.mult, ALU.add))
                        p.S("scalar", lambda s, o: s.activation(
                            t2_5[:, :], t2_5[:, :], AF.Tanh, scale=0.7978845608028654))
                        p.S("vector", lambda v, o: v.tensor_scalar_add(t2_5[:, :],
                                                                       t2_5[:, :], 1.0))
                        p.S("vector", lambda v, o, fc=fc: v.scalar_tensor_tensor(
                            HeT[:, fc * 512:(fc + 1) * 512], t1_5[:, :], 0.5,
                            t2_5[:, :], ALU.mult, ALU.mult))
                    for dc in range(8):
                        mm_acc(p, pa[:, :],
                               lambda fc, dc=dc: bass.AP(
                                   AR2, fc * 1024 + dc * 128, [[8192, 128], [1, 128]]),
                               lambda fc: HeT[:, fc * 512:(fc + 1) * 512], 8)
                        p.dma("sync", lambda o: accld[:, :],
                              lambda o, dc=dc: bass.AP(
                                  accT2, o([(tc, 512)], dc * 128 * TPC),
                                  [[TPC, 128], [1, 512]]))
                        p.S("vector", lambda v, o, dc=dc: v.tensor_scalar(
                            t1_5[:, :], pa[:, :], b2cq[:, dc:dc + 1], None, ALU.add))
                        if use_gate:
                            p.S("vector", lambda v, o: v.tensor_mul(t1_5[:, :],
                                                                    t1_5[:, :],
                                                                    combbc[:, :]))
                        p.S("vector", lambda v, o: v.tensor_add(accld[:, :], accld[:, :],
                                                                t1_5[:, :]))
                        p.dma("gpsimd",
                              lambda o, dc=dc: bass.AP(
                                  accT2, o([(tc, 512)], dc * 128 * TPC),
                                  [[TPC, 128], [1, 512]]),
                              lambda o: accld[:, :])
                p.loop(8, tcb)
            p.loop(4, fqb)

        p.loop(8, lambda e: expert_block(
            True, W1_d, b1_d, W2_d, b2_d, e,
            (1024 * 4096, 4096 * 1024, 4096, 1024)))
        expert_block(False, Ws1_d, bs1_d, Ws2_d, bs2_d, 0, (0, 0, 0, 0))

        def fin(i):
            for j in range(8):
                p.dma("sync", lambda o, j=j: accld[:, 0:128],
                      lambda o, j=j: bass.AP(accT2, o([(i, 128)], j * 128 * TPC),
                                             [[TPC, 128], [1, 128]]))
                p.S("tensor", lambda t, o: t.transpose(ptf[:, :], accld[:, 0:128],
                                                       id_f[:, :]))
                p.S("vector", lambda v, o, j=j: v.tensor_copy(
                    t_b[:, j * 128:(j + 1) * 128], ptf[:, :]))
            p.dma("sync", lambda o: t_a[:, :],
                  lambda o: bass.AP(s1_d, o([(i, 128 * 1024)]), [[1024, 128], [1, 1024]]))
            p.dma("sync", lambda o: t_c[:, :],
                  lambda o: bass.AP(x1_d, o([(i, 128 * 1024)]), [[1024, 128], [1, 1024]]))
            p.S("scalar", lambda s, o: s.activation(t_b[:, :], t_b[:, :], AF.Silu))
            p.S("vector", lambda v, o: v.tensor_mul(t_b[:, :], t_b[:, :], t_a[:, :]))
            p.S("vector", lambda v, o: v.tensor_add(t_b[:, :], t_b[:, :], t_c[:, :]))
            p.dma("gpsimd", lambda o: bass.AP(out_d, o([(i, 128 * 1024)]),
                                              [[1024, 128], [1, 1024]]),
                  lambda o: t_b[:, :])
        p.loop(TILES, fin)
        p.final_wait("gpsimd")
    nc.finalize()
    return nc


def run_l1ffn(ld, s1, X):
    nc = build_l1ffn()
    common = {
        "scf_i": np.ascontiguousarray(ld["scores"]),
        "W1_i": X["L1_ffn_W1"], "b1_i": X["L1_ffn_b1"],
        "W2_i": X["L1_ffn_W2"], "b2_i": X["L1_ffn_b2"],
        "Ws1_i": X["L1_ffn_Ws1"],
        "bs1_i": np.ascontiguousarray(X["L1_ffn_bs1"].reshape(1, 4096)),
        "Ws2_i": X["L1_ffn_Ws2"],
        "bs2_i": np.ascontiguousarray(X["L1_ffn_bs2"].reshape(1, 1024)),
    }
    in_maps = []
    for c in range(NCORES):
        b, q = c // 4, c % 4
        sl = slice(q * TPC, (q + 1) * TPC)
        m = dict(common)
        m["xT_i"] = np.ascontiguousarray(ld["xn1T"][b][:, sl])
        m["scm_i"] = np.ascontiguousarray(ld["scores"][c * TPC:(c + 1) * TPC])
        m["s1_i"] = np.ascontiguousarray(s1[b, sl])
        m["x1_i"] = np.ascontiguousarray(ld["x1"][b, sl])
        in_maps.append(m)
    res = run_bass_kernel_spmd(nc, in_maps, core_ids=list(range(NCORES)))
    R = res.results
    out = np.stack([np.stack([R[b * 4 + q]["out_o"] for q in range(4)])
                    for b in range(2)]).reshape(2, 16384, 1024)
    return out


def kernel(**X):
    X = {k: np.ascontiguousarray(v) for k, v in X.items()}
    lb = run_l0front(X)
    yv = run_wkv(lb, X)
    ld = run_l0post(lb, yv, X)
    s1 = run_l1attn(ld, X)
    out = run_l1ffn(ld, s1, X)
    return out
